# version 08 (final)
# speedup vs baseline: 1.9542x; 1.0933x over previous
import numpy as np
import ml_dtypes
import concourse.bass as bass
import concourse.mybir as mybir
from concourse.bass_utils import run_bass_kernel_spmd
from contextlib import ExitStack

F32 = mybir.dt.float32
BF16 = mybir.dt.bfloat16
I32 = mybir.dt.int32
AF = mybir.ActivationFunctionType
ALU = mybir.AluOpType
AX = mybir.AxisListType

BF = ml_dtypes.bfloat16

DEBUG_DUMP_SB = False
B, N, D, H, HD = 2, 4096, 2048, 16, 128
SPAN = 1024
CH = 128
NCH = SPAN // CH
NCORE = 8
DI, DS, DTR = 2048, 32, 64
D2 = 256


ENGS = ("sync", "tensor", "vector", "scalar", "gpsimd")


class Ctr:
    __slots__ = ("h", "n", "name")

    def __init__(self, h, name):
        self.h = h
        self.n = 0
        self.name = name


class Sched:

    def __init__(self, nc, stack):
        self.nc = nc
        self.stack = stack
        self.ops = {k: [] for k in ENGS}
        self.last = {}

    def sem(self, name):
        return Ctr(self.stack.enter_context(self.nc.semaphore(name)), name)

    def op(self, eng, build, waits=(), inc=None, inc_k=1, is_dma=False):
        wl = []
        seen = {}
        for w in list(waits) + ([self.last[eng]] if (not is_dma and eng in self.last) else []):
            c, v = (w if isinstance(w, tuple) else (w, w.n))
            if v > 0:
                seen[c.h] = max(seen.get(c.h, 0), v)
        wl = list(seen.items())
        if inc is not None:
            inc.n += inc_k
            if not is_dma and eng in ("vector", "scalar", "gpsimd"):
                self.last[eng] = (inc, inc.n)

        def run(e, wl=wl, build=build, inc=inc, inc_k=inc_k):
            for h, v in wl:
                e.wait_ge(h, v)
            inst = build(e)
            if inc is not None:
                assert inst is not None
                inst.then_inc(inc.h, inc_k)

        self.ops[eng].append(run)

    def dma(self, eng, out, in_, waits=(), inc=None):
        assert inc is not None
        self.op(eng, lambda e, o=out, i=in_: e.dma_start(out=o, in_=i),
                waits=waits, inc=inc, inc_k=16, is_dma=True)

    def wait(self, eng, waits):
        self.op(eng, lambda e: None, waits=waits)

    def finalize(self, block):
        def mk(k):
            def f(eng):
                for run in self.ops[k]:
                    run(eng)
            return f
        block.sync(mk("sync"))
        block.tensor(mk("tensor"))
        block.vector(mk("vector"))
        block.scalar(mk("scalar"))
        block.gpsimd(mk("gpsimd"))


def fm_pair(sb_tile, dram_ap, p=128):
    kk = dram_ap.shape[0] // p
    t = dram_ap.shape[1]
    return (sb_tile[:, :].rearrange("p (k t) -> p k t", k=kk),
            dram_ap.rearrange("(k p) t -> p k t", p=p))



def build_launch_A():
    nc = bass.Bass("TRN2", target_bir_lowering=False)
    xfm_d = nc.dram_tensor("xfm", [D, SPAN], F32, kind="ExternalInput").ap()
    wqk_d = nc.dram_tensor("Wqkp", [H * 128, 2 * D], F32, kind="ExternalInput").ap()
    wv_d = nc.dram_tensor("Wvp", [4 * 128, 16 * 512], F32, kind="ExternalInput").ap()
    ld_d = nc.dram_tensor("ld_row", [1, H], F32, kind="ExternalInput").ap()
    idx1_d = nc.dram_tensor("idx1_row", [1, CH], F32, kind="ExternalInput").ap()
    rev_d = nc.dram_tensor("rev_col", [CH, 1], F32, kind="ExternalInput").ap()
    trid_d = nc.dram_tensor("tridT", [CH, CH], F32, kind="ExternalInput").ap()
    ones_f_d = nc.dram_tensor("ones_f", [1, CH], F32, kind="ExternalInput").ap()
    ones_b_d = nc.dram_tensor("ones_b", [1, CH], BF16, kind="ExternalInput").ap()
    ident_d = nc.dram_tensor("ident_b", [CH, CH], BF16, kind="ExternalInput").ap()

    slocal_d = nc.dram_tensor("s_local", [HD, D], F32, kind="ExternalOutput").ap()
    dbg_d = nc.dram_tensor("dbg", [HD, NCH * HD], F32, kind="ExternalOutput").ap()
    dbg2_d = nc.dram_tensor("dbg2", [HD, 4 * NCH * HD], F32, kind="ExternalOutput").ap()
    oloc_d = nc.dram_tensor("o_loc", [SPAN, D], BF16, kind="ExternalOutput").ap()
    qs_d = nc.dram_tensor("qs", [HD, H * SPAN], BF16, kind="ExternalOutput").ap()

    with ExitStack() as st:
        sb = lambda n, s, d: st.enter_context(nc.sbuf_tensor(n, s, d))
        psf = lambda n, w=512: st.enter_context(nc.psum_tensor(n, [128, w], F32))
        xfm = sb("xfm_s", [128, 16 * SPAN], BF16)
        vtm = sb("vtm", [128, NCH * D], BF16)
        wvb = [sb(f"wvb{i}", [128, 16 * 512], BF16) for i in range(2)]
        wqkc = [sb(f"wqkc{i}", [128, 2 * 16 * 128], BF16) for i in range(2)]
        qh = sb("qh", [128, SPAN], BF16)
        kh = sb("kh", [128, SPAN], BF16)
        qsh = sb("qsh", [128, SPAN], BF16)
        osb = sb("osb", [128, SPAN], BF16)
        scm = sb("scm", [128, CH], BF16)
        kdT = sb("kdT", [128, CH], BF16)
        Sf = [sb(f"Sf{i}", [128, HD], F32) for i in range(2)]
        Sb = sb("Sb", [128, HD], BF16)
        ld_row = sb("ld_row_s", [1, H], F32)
        idx1 = sb("idx1_s", [1, CH], F32)
        rev = sb("rev_s", [CH, 1], F32)
        trid = sb("trid_s", [CH, CH], F32)
        ones_f = sb("ones_f_s", [1, CH], F32)
        ones_b = sb("ones_b_s", [1, CH], BF16)
        ident = sb("ident_s", [CH, CH], BF16)
        ldbc = sb("ldbc", [128, H], F32)
        kdtmp = sb("kdtmp", [128, H], F32)
        kdec = sb("kdec", [128, H], F32)
        cdc = sb("cdc", [128, H], F32)
        qdrow = sb("qdrow", [1, H * CH], BF16)
        qdbc = sb("qdbc", [128, CH], BF16)
        DtT = sb("DtT", [128, CH], BF16)
        dbgs = sb("dbgs", [128, NCH * HD], F32)
        dbg2s = sb("dbg2s", [128, 4 * NCH * HD], F32)

        pQ = psf("pQ")
        pV = [psf(f"pV{i}") for i in range(2)]
        pSc = [psf(f"pSc{i}") for i in range(2)]
        pKT = st.enter_context(nc.psum_tensor("pKT", [128, 512], BF16))
        pO = psf("pO")
        pS = psf("pS")
        pB = pQ

        sc = Sched(nc, st)
        c_ld = sc.sem("c_ld")
        c_x = sc.sem("c_x")
        c_wv = [sc.sem("c_wv0"), sc.sem("c_wv1")]
        c_wqk = [sc.sem("c_wqk0"), sc.sem("c_wqk1")]
        c_pe = sc.sem("c_pe")
        c_dve = sc.sem("c_dve")
        c_act = sc.sem("c_act")
        c_st_qs = sc.sem("c_st_qs")
        c_st_o = sc.sem("c_st_o")
        c_st_s = sc.sem("c_st_s")
        block = st.enter_context(nc.Block())

        for out, i in ((ld_row, ld_d), (idx1, idx1_d), (rev, rev_d),
                       (trid, trid_d), (ones_f, ones_f_d), (ones_b, ones_b_d),
                       (ident, ident_d)):
            sc.dma("sync", out[:, :], i[:, :], inc=c_ld)
        lds = c_ld.n

        sc.dma("gpsimd", *fm_pair(xfm, xfm_d), inc=c_x)
        xfm_rdy = c_x.n

        if DEBUG_DUMP_SB:
            sc.op("vector", lambda e: e.memset(dbg2s[:, :], 0.0), inc=c_dve)
            sc.op("vector", lambda e: e.memset(dbgs[:, :], 0.0), inc=c_dve)
        sc.op("tensor", lambda e: e.matmul(pQ[:, 0:H], ones_f[:, :], ld_row[:, :],
                                           start=True, stop=True),
              waits=[(c_ld, lds)], inc=c_pe)
        sc.op("vector", lambda e: e.tensor_copy(ldbc[:, :], pQ[:, 0:H]),
              waits=[c_pe], inc=c_dve)
        ldbc_rdy = c_dve.n
        sc.op("vector", lambda e: e.tensor_scalar(kdtmp[:, :], ldbc[:, :],
                                                  rev[:, 0:1], None, ALU.mult),
              waits=[(c_ld, lds)], inc=c_dve)
        sc.op("scalar", lambda e: e.activation(kdec[:, :], kdtmp[:, :], AF.Exp),
              waits=[c_dve], inc=c_act)
        sc.op("scalar", lambda e: e.activation(cdc[:, :], ldbc[:, :], AF.Exp,
                                               scale=float(CH)),
              waits=[(c_dve, ldbc_rdy)], inc=c_act)
        for h in range(H):
            sc.op("scalar",
                  lambda e, h=h: e.activation(qdrow[0:1, h * CH:(h + 1) * CH],
                                              idx1[0:1, :], AF.Exp,
                                              scale=ldbc[0:1, h:h + 1]),
                  waits=[(c_dve, ldbc_rdy), (c_ld, lds)], inc=c_act)
        qdrow_rdy = c_act.n

        vidx_act = {}
        for nb in range(4):
            buf = wvb[nb % 2]
            cw = c_wv[nb % 2]
            wts = [(c_act, vidx_act[(nb - 2) * NCH + NCH - 1])] if nb >= 2 else []
            sc.dma("gpsimd", buf[:, :], wv_d[nb * 128:(nb + 1) * 128, :],
                   waits=wts + [(c_x, xfm_rdy)], inc=cw)
            wrdy = cw.n
            for i in range(NCH):
                ps = pV[i % 2]
                idx = nb * NCH + i
                pewaits = [(cw, wrdy)]
                if idx >= 2:
                    pewaits.append((c_act, vidx_act[idx - 2]))
                for k in range(16):
                    sc.op("tensor",
                          lambda e, k=k, i=i, ps=ps, buf=buf:
                          e.matmul(ps[:, :],
                                   xfm[:, k * SPAN + i * CH:k * SPAN + (i + 1) * CH],
                                   buf[:, k * 512:(k + 1) * 512],
                                   start=(k == 0), stop=(k == 15)),
                          waits=pewaits if k == 0 else [], inc=c_pe)
                sc.op("scalar",
                      lambda e, i=i, nb=nb, ps=ps:
                      e.activation(vtm[:, i * D + nb * 512:i * D + (nb + 1) * 512],
                                   ps[:, :], AF.Silu),
                      waits=[c_pe], inc=c_act)
                vidx_act[idx] = c_act.n

        qk_free = {}
        qs_store = {}
        o_store = {}
        s_store = {}
        for h in range(H):
            bqk = wqkc[h % 2]
            wts = [(c_pe, qk_free[h - 2])] if h >= 2 else []
            sc.dma("gpsimd", bqk[:, :], wqk_d[h * 128:(h + 1) * 128, :],
                   waits=wts, inc=c_wqk[h % 2])
            wrdy = c_wqk[h % 2].n
            for qoff, dst in ((0, qh), (2048, kh)):
                for nb2 in range(2):
                    for k in range(16):
                        w0 = []
                        if k == 0:
                            w0 = [(c_wqk[h % 2], wrdy), (c_act, c_act.n),
                                  (c_dve, c_dve.n), (c_x, xfm_rdy)]
                        sc.op("tensor",
                              lambda e, k=k, nb2=nb2, qoff=qoff, bqk=bqk:
                              e.matmul(pQ[:, :],
                                       bqk[:, qoff + k * 128:qoff + (k + 1) * 128],
                                       xfm[:, k * SPAN + nb2 * 512:
                                           k * SPAN + (nb2 + 1) * 512],
                                       start=(k == 0), stop=(k == 15)),
                              waits=w0, inc=c_pe)
                    sc.op("scalar",
                          lambda e, nb2=nb2, dst=dst:
                          e.activation(dst[:, nb2 * 512:(nb2 + 1) * 512],
                                       pQ[:, :], AF.Silu),
                          waits=[c_pe], inc=c_act)
            qk_free[h] = c_pe.n
            qk_rdy = c_act.n
            sc.op("tensor",
                  lambda e, h=h: e.matmul(pB[:, 0:CH], ones_b[:, :],
                                          qdrow[0:1, h * CH:(h + 1) * CH],
                                          start=True, stop=True),
                  waits=[(c_act, max(qdrow_rdy, c_act.n)), (c_dve, c_dve.n),
                         (c_ld, lds)],
                  inc=c_pe)
            sc.op("vector", lambda e: e.tensor_copy(qdbc[:, :], pB[:, 0:CH]),
                  waits=[c_pe], inc=c_dve)
            qdbc_rdy = c_dve.n
            sc.op("scalar",
                  lambda e, h=h: e.activation(DtT[:, :], trid[:, :], AF.Exp,
                                              scale=ldbc[:, h:h + 1]),
                  waits=[(c_ld, lds), (c_dve, c_dve.n)], inc=c_act)
            dt_rdy = c_act.n
            for i in range(NCH):
                w0 = [(c_dve, qdbc_rdy), (c_act, qk_rdy)]
                if i == 0 and h >= 1:
                    w0.append((c_st_qs, qs_store[h - 1]))
                sc.op("vector",
                      lambda e, i=i: e.tensor_tensor(
                          qsh[:, i * CH:(i + 1) * CH],
                          qh[:, i * CH:(i + 1) * CH], qdbc[:, :], ALU.mult),
                      waits=w0, inc=c_dve)
            sc.dma("sync", qs_d[:, h * SPAN:(h + 1) * SPAN], qsh[:, :],
                   waits=[c_dve], inc=c_st_qs)
            qs_store[h] = c_st_qs.n
            sc.op("vector", lambda e: e.memset(Sf[0][:, :], 0.0),
                  waits=([(c_st_s, s_store[h - 1])] if h >= 1 else []), inc=c_dve)
            sc.op("vector", lambda e: e.memset(Sb[:, :], 0.0), inc=c_dve)
            for i in range(NCH):
                ping, pong = Sf[i % 2], Sf[(i + 1) % 2]
                psc = pSc[i % 2]
                sc.op("tensor",
                      lambda e, i=i, psc=psc: e.matmul(
                          psc[:, 0:CH], kh[:, i * CH:(i + 1) * CH],
                          qh[:, i * CH:(i + 1) * CH], start=True, stop=True),
                      waits=[(c_act, qk_rdy), (c_dve, c_dve.n)], inc=c_pe)
                pe_sc = c_pe.n
                sc.op("tensor",
                      lambda e, i=i: e.transpose(
                          pKT[:, 0:CH], kh[:, i * CH:(i + 1) * CH], ident[:, :]),
                      waits=[(c_ld, lds), (c_dve, c_dve.n)], inc=c_pe)
                pe_kt = c_pe.n
                sc.op("vector",
                      lambda e, psc=psc: e.tensor_tensor(
                          scm[:, :], psc[:, 0:CH], DtT[:, :], ALU.mult),
                      waits=[(c_pe, pe_sc), (c_act, dt_rdy)], inc=c_dve)
                scm_rdy = c_dve.n
                sc.op("vector",
                      lambda e, h=h: e.tensor_scalar(
                          kdT[:, :], pKT[:, 0:CH], kdec[:, h:h + 1], None,
                          ALU.mult),
                      waits=[(c_pe, pe_kt), (c_act, c_act.n)], inc=c_dve)
                kdt_rdy = c_dve.n
                w_o = [(c_dve, scm_rdy)]
                if i == 0 and h >= 1:
                    w_o.append((c_st_o, o_store[h - 1]))
                sc.op("tensor",
                      lambda e, i=i, h=h: e.matmul(
                          pO[:, 0:HD], scm[:, :],
                          vtm[:, i * D + h * HD:i * D + (h + 1) * HD],
                          start=True, stop=False),
                      waits=w_o, inc=c_pe)
                if h == min(15, H - 1) and DEBUG_DUMP_SB:
                    sc.op("vector",
                          lambda e, i=i: e.tensor_copy(
                              dbgs[:, i * HD:(i + 1) * HD], Sb[:, :]),
                          inc=c_dve)
                sc.op("tensor",
                      lambda e, i=i: e.matmul(
                          pO[:, 0:HD], qsh[:, i * CH:(i + 1) * CH], Sb[:, :],
                          start=False, stop=True),
                      waits=[(c_dve, c_dve.n)], inc=c_pe)
                pe_o = c_pe.n
                sc.op("tensor",
                      lambda e, i=i, h=h: e.matmul(
                          pS[:, 0:HD], kdT[:, :],
                          vtm[:, i * D + h * HD:i * D + (h + 1) * HD],
                          start=True, stop=True),
                      waits=[(c_dve, kdt_rdy)], inc=c_pe)
                pe_s = c_pe.n
                sc.op("vector",
                      lambda e, i=i: e.tensor_copy(
                          osb[:, i * CH:(i + 1) * CH], pO[:, 0:HD]),
                      waits=[(c_pe, pe_o)], inc=c_dve)
                sc.op("vector",
                      lambda e, h=h, ping=ping, pong=pong:
                      e.scalar_tensor_tensor(
                          pong[:, :], ping[:, :], cdc[:, h:h + 1], pS[:, 0:HD],
                          ALU.mult, ALU.add),
                      waits=[(c_pe, pe_s), (c_act, c_act.n)], inc=c_dve)
                sc.op("vector",
                      lambda e, pong=pong: e.tensor_copy(Sb[:, :], pong[:, :]),
                      inc=c_dve)
            sc.dma("sync",
                   oloc_d[:, h * HD:(h + 1) * HD].rearrange(
                       "(i p) e -> p i e", p=CH),
                   osb[:, :].rearrange("p (i e) -> p i e", i=NCH),
                   waits=[c_dve], inc=c_st_o)
            sc.dma("sync", slocal_d[:, h * HD:(h + 1) * HD], Sf[0][:, :],
                   waits=[c_dve], inc=c_st_s)
            o_store[h] = c_st_o.n
            s_store[h] = c_st_s.n
        if DEBUG_DUMP_SB:
            sc.dma("sync", dbg_d[:, :], dbgs[:, :], waits=[c_dve], inc=c_st_s)
            sc.dma("sync", dbg2_d[:, :], dbg2s[:, :], waits=[c_dve], inc=c_st_s)
        sc.wait("sync", [(c_st_qs, c_st_qs.n), (c_st_o, c_st_o.n),
                         (c_st_s, c_st_s.n)])
        sc.finalize(block)
    return nc



EPS = 1e-5


def build_launch_B1():
    nc = bass.Bass("TRN2", target_bir_lowering=False)
    x_d = nc.dram_tensor("x_tm", [SPAN, D], F32, kind="ExternalInput").ap()
    xfm_d = nc.dram_tensor("xfm", [D, SPAN], F32, kind="ExternalInput").ap()
    ol_d = nc.dram_tensor("o_loc", [SPAN, D], BF16, kind="ExternalInput").ap()
    qs_d = nc.dram_tensor("qs", [HD, H * SPAN], BF16, kind="ExternalInput").ap()
    sall_d = nc.dram_tensor("s_all", [NCORE * HD, D], F32, kind="ExternalInput").ap()
    kvec_d = nc.dram_tensor("kvec", [1, NCORE], F32, kind="ExternalInput").ap()
    ld_d = nc.dram_tensor("ld_row", [1, H], F32, kind="ExternalInput").ap()
    wg1_d = nc.dram_tensor("Wg1p", [128, 16 * HD], F32, kind="ExternalInput").ap()
    wg2_d = nc.dram_tensor("Wg2", [HD, D], F32, kind="ExternalInput").ap()
    nw_d = nc.dram_tensor("normw_fm", [128, 16], F32, kind="ExternalInput").ap()
    wo_d = nc.dram_tensor("Wop", [4 * 128, 16 * 512], F32, kind="ExternalInput").ap()
    ones_f_d = nc.dram_tensor("ones_f", [1, CH], F32, kind="ExternalInput").ap()
    ident_d = nc.dram_tensor("ident_b", [CH, CH], BF16, kind="ExternalInput").ap()
    x1_d = nc.dram_tensor("x1", [SPAN, D], F32, kind="ExternalOutput").ap()

    with ExitStack() as st:
        sb = lambda n, s, d: st.enter_context(nc.sbuf_tensor(n, s, d))
        psf = lambda n, w=512: st.enter_context(nc.psum_tensor(n, [128, w], F32))
        qsc = [sb(f"qsc{i}", [128, H * CH], BF16) for i in range(2)]
        xk = [sb(f"xk{i}", [128, SPAN], BF16) for i in range(2)]
        o4fm = sb("o4fm", [128, NCH * D], BF16)
        wob = sb("wob", [128, 16 * 512], BF16)
        s_r = [sb(f"s_r{i}", [128, D], F32) for i in range(2)]
        Sinb = sb("Sinb", [128, D], BF16)
        olt = [sb(f"olt{i}", [128, D], BF16) for i in range(2)]
        o2 = sb("o2", [128, D], F32)
        sg = sb("sg", [128, D], BF16)
        o3 = sb("o3", [128, D], F32)
        sq = sb("sq", [128, D], F32)
        o4 = sb("o4", [128, D], BF16)
        g1 = sb("g1", [128, SPAN], BF16)
        wg1s = sb("wg1s", [128, 16 * HD], BF16)
        wg2s = sb("wg2s", [HD, D], BF16)
        x1pre = sb("x1pre", [128, NCH * D], BF16)
        xb = [sb(f"xb{i}", [128, 512], F32) for i in range(2)]
        x1o = sb("x1o", [128, D], F32)
        ld_row = sb("ld_s", [1, H], F32)
        kvec = sb("kvec_s", [1, NCORE], F32)
        ones_f = sb("ones_s", [1, CH], F32)
        nwf = sb("nwf_s", [128, 16], F32)
        ident = sb("ident_s", [CH, CH], BF16)
        ldbc = sb("ldbc", [128, H], F32)
        kbc = sb("kbc", [128, NCORE], F32)
        wexm = sb("wexm", [128, H], F32)
        wex = sb("wex", [128, NCORE * H], F32)
        cdp = sb("cdp", [128, H], F32)
        sums = sb("sums", [128, H], F32)
        sdc = sb("sdc", [128, H], F32)
        rstd = sb("rstd", [128, H], F32)
        mcol = sb("mcol", [128, 1], F32)
        m2col = sb("m2col", [128, 1], F32)
        vcol = sb("vcol", [128, 1], F32)
        sdl = sb("sdl", [128, 1], F32)
        rsl = sb("rsl", [128, 1], F32)
        scol = sb("scol", [128, 1], F32)
        s2col = sb("s2col", [128, 1], F32)
        epsc = sb("epsc", [128, 1], F32)

        pB = psf("pB")
        pSa = st.enter_context(nc.psum_tensor("pSa", [128, D], F32))
        pG1 = [psf(f"pG1{i}") for i in range(2)]
        pT = st.enter_context(nc.psum_tensor("pT", [128, 512], BF16))

        sc = Sched(nc, st)
        c_ld = sc.sem("c_ld")
        c_x = sc.sem("c_x")
        c_x2 = sc.sem("c_x2")
        c_qs = [sc.sem("c_qs0"), sc.sem("c_qs1")]
        c_sr = [sc.sem("c_sr0"), sc.sem("c_sr1")]
        c_ol = [sc.sem("c_ol0"), sc.sem("c_ol1")]
        c_wg = sc.sem("c_wg")
        c_wo = sc.sem("c_wo")
        c_xb = [sc.sem("c_xb0"), sc.sem("c_xb1")]
        c_pe = sc.sem("c_pe")
        c_dve = sc.sem("c_dve")
        c_act = sc.sem("c_act")
        c_st = sc.sem("c_st")
        block = st.enter_context(nc.Block())

        for out, i in ((ld_row, ld_d), (kvec, kvec_d), (ones_f, ones_f_d),
                       (nwf, nw_d), (ident, ident_d)):
            sc.dma("sync", out[:, :], i[:, :], inc=c_ld)
        lds = c_ld.n
        sc.dma("gpsimd", wg1s[:, :], wg1_d[:, :], inc=c_wg)
        sc.dma("gpsimd", wg2s[:, :], wg2_d[:, :], inc=c_wg)
        wg_rdy = c_wg.n

        sc.op("vector", lambda e: e.memset(epsc[:, :], EPS), inc=c_dve)
        sc.op("tensor", lambda e: e.matmul(pB[:, 0:H], ones_f[:, :],
                                           ld_row[:, :], start=True, stop=True),
              waits=[(c_ld, lds)], inc=c_pe)
        sc.op("tensor", lambda e: e.matmul(pB[:, 16:16 + NCORE], ones_f[:, :],
                                           kvec[:, :], start=True, stop=True),
              inc=c_pe)
        sc.op("vector", lambda e: e.tensor_copy(ldbc[:, :], pB[:, 0:H]),
              waits=[c_pe], inc=c_dve)
        sc.op("vector", lambda e: e.tensor_copy(kbc[:, :], pB[:, 16:16 + NCORE]),
              inc=c_dve)
        ldbc_rdy = c_dve.n
        for r in range(NCORE):
            sc.op("vector",
                  lambda e, r=r: e.tensor_scalar(wexm[:, :], ldbc[:, :],
                                                 kbc[:, r:r + 1], None, ALU.mult),
                  waits=[(c_act, c_act.n)], inc=c_dve)
            sc.op("scalar",
                  lambda e, r=r: e.activation(wex[:, r * H:(r + 1) * H],
                                              wexm[:, :], AF.Exp),
                  waits=[c_dve], inc=c_act)
        wex_rdy = c_act.n

        for r in range(NCORE):
            buf = s_r[r % 2]
            wts = [(c_dve, c_dve.n)] if r >= 2 else []
            sc.dma("sync", buf[:, :], sall_d[r * HD:(r + 1) * HD, :],
                   waits=wts, inc=c_sr[r % 2])
            srdy = c_sr[r % 2].n
            for h in range(H):
                if r == 0:
                    sc.op("vector",
                          lambda e, h=h, buf=buf: e.tensor_scalar(
                              pSa[:, h * HD:(h + 1) * HD],
                              buf[:, h * HD:(h + 1) * HD],
                              wex[:, h:h + 1], None, ALU.mult),
                          waits=([(c_sr[0], srdy), (c_act, wex_rdy),
                                  (c_pe, c_pe.n)] if h == 0 else []),
                          inc=c_dve)
                else:
                    sc.op("vector",
                          lambda e, h=h, r=r, buf=buf: e.scalar_tensor_tensor(
                              pSa[:, h * HD:(h + 1) * HD],
                              buf[:, h * HD:(h + 1) * HD],
                              wex[:, r * H + h:r * H + h + 1],
                              pSa[:, h * HD:(h + 1) * HD],
                              ALU.mult, ALU.add),
                          waits=([(c_sr[r % 2], srdy)] if h == 0 else []),
                          inc=c_dve)
        sc.op("vector", lambda e: e.tensor_copy(Sinb[:, :], pSa[:, :]),
              inc=c_dve)
        sin_rdy = c_dve.n

        for k in range(16):
            buf = xk[k % 2]
            wts = [(c_pe, c_pe.n)] if k >= 2 else []
            sc.dma("gpsimd", buf[:, :],
                   xfm_d[k * 128:(k + 1) * 128, :], waits=wts,
                   inc=c_x if k % 2 == 0 else c_x2)
            xrdy = (c_x if k % 2 == 0 else c_x2).n
            for nb2 in range(2):
                sc.op("tensor",
                      lambda e, k=k, nb2=nb2, buf=buf: e.matmul(
                          pG1[nb2][:, :], wg1s[:, k * HD:(k + 1) * HD],
                          buf[:, nb2 * 512:(nb2 + 1) * 512],
                          start=(k == 0), stop=(k == 15)),
                      waits=([(c_x if k % 2 == 0 else c_x2, xrdy),
                              (c_wg, wg_rdy)] if nb2 == 0 else []),
                      inc=c_pe)
        for nb2 in range(2):
            sc.op("scalar",
                  lambda e, nb2=nb2: e.activation(
                      g1[:, nb2 * 512:(nb2 + 1) * 512], pG1[nb2][:, :],
                      AF.Copy),
                  waits=[c_pe], inc=c_act)
        g1_rdy = c_act.n

        for i in range(NCH):
            bq = qsc[i % 2]
            bo = olt[i % 2]
            wts = [(c_pe, c_pe.n)] if i >= 2 else []
            sc.dma("sync",
                   bq[:, :].rearrange("p (h t) -> p h t", h=H),
                   qs_d.rearrange("p (h t) -> p h t", h=H)
                   [:, :, i * CH:(i + 1) * CH],
                   waits=wts, inc=c_qs[i % 2])
            qrdy = c_qs[i % 2].n
            wts2 = [(c_dve, c_dve.n)] if i >= 2 else []
            sc.dma("sync", bo[:, :], ol_d[i * CH:(i + 1) * CH, :],
                   waits=wts2, inc=c_ol[i % 2])
            ordy = c_ol[i % 2].n
            sc.op("scalar",
                  lambda e, i=i: e.activation(cdp[:, :], ldbc[:, :], AF.Exp,
                                              scale=float(CH * i)),
                  waits=[(c_dve, ldbc_rdy), (c_dve, c_dve.n)], inc=c_act)
            cdp_rdy = c_act.n
            for h in range(H):
                sc.op("tensor",
                      lambda e, h=h, bq=bq: e.matmul(
                          pB[:, 0:HD], bq[:, h * CH:(h + 1) * CH],
                          Sinb[:, h * HD:(h + 1) * HD], start=True, stop=True),
                      waits=([(c_qs[i % 2], qrdy), (c_dve, sin_rdy),
                              (c_act, cdp_rdy), (c_dve, c_dve.n)]
                             if h == 0 else [(c_dve, c_dve.n)]),
                      inc=c_pe)
                sc.op("vector",
                      lambda e, h=h, bo=bo: e.scalar_tensor_tensor(
                          o2[:, h * HD:(h + 1) * HD], pB[:, 0:HD],
                          cdp[:, h:h + 1], bo[:, h * HD:(h + 1) * HD],
                          ALU.mult, ALU.add),
                      waits=[c_pe, (c_ol[i % 2], ordy)], inc=c_dve)
            for nb in range(4):
                sc.op("tensor",
                      lambda e, i=i, nb=nb: e.matmul(
                          pB[:, 0:512], g1[:, i * CH:(i + 1) * CH],
                          wg2s[:, nb * 512:(nb + 1) * 512],
                          start=True, stop=True),
                      waits=[(c_act, max(g1_rdy, c_act.n)), (c_dve, c_dve.n),
                             (c_wg, wg_rdy)],
                      inc=c_pe)
                sc.op("scalar",
                      lambda e, nb=nb: e.activation(
                          sg[:, nb * 512:(nb + 1) * 512], pB[:, 0:512],
                          AF.Sigmoid),
                      waits=[c_pe], inc=c_act)
            sc.op("vector", lambda e: e.tensor_tensor(o3[:, :], o2[:, :],
                                                      sg[:, :], ALU.mult),
                  waits=[c_act], inc=c_dve)
            sc.op("scalar", lambda e: e.activation(sq[:, :], o3[:, :],
                                                   AF.Square),
                  waits=[c_dve], inc=c_act)
            sc.op("vector",
                  lambda e: e.reduce_sum(
                      sums[:, :], sq[:, :].rearrange("p (g e) -> p g e", g=H),
                      axis=AX.X),
                  waits=[c_act], inc=c_dve)
            sc.op("scalar",
                  lambda e: e.activation(sdc[:, :], sums[:, :], AF.Sqrt,
                                         bias=epsc[:, 0:1], scale=1.0 / HD),
                  waits=[c_dve], inc=c_act)
            sc.op("vector", lambda e: e.reciprocal(rstd[:, :], sdc[:, :]),
                  waits=[c_act], inc=c_dve)
            for h in range(H):
                sc.op("vector",
                      lambda e, h=h: e.tensor_scalar(
                          o4[:, h * HD:(h + 1) * HD], o3[:, h * HD:(h + 1) * HD],
                          rstd[:, h:h + 1], None, ALU.mult),
                      inc=c_dve)
            for k in range(16):
                sc.op("tensor",
                      lambda e, k=k: e.transpose(
                          pT[:, 0:CH], o4[:, k * HD:(k + 1) * HD], ident[:, :]),
                      waits=[(c_dve, c_dve.n), (c_ld, lds)], inc=c_pe)
                sc.op("vector",
                      lambda e, i=i, k=k: e.tensor_scalar(
                          o4fm[:, i * D + k * HD:i * D + (k + 1) * HD],
                          pT[:, 0:CH], nwf[:, k:k + 1], None, ALU.mult),
                      waits=[c_pe], inc=c_dve)

        for nb in range(4):
            wts = [(c_pe, c_pe.n)] if nb >= 1 else []
            sc.dma("gpsimd", wob[:, :], wo_d[nb * 128:(nb + 1) * 128, :],
                   waits=wts, inc=c_wo)
            wrdy = c_wo.n
            for i in range(NCH):
                bx = xb[i % 2]
                wts2 = [(c_dve, c_dve.n)] if (nb * NCH + i) >= 2 else []
                sc.dma("sync", bx[:, :],
                       x_d[i * CH:(i + 1) * CH, nb * 512:(nb + 1) * 512],
                       waits=wts2, inc=c_xb[i % 2])
                xrdy = c_xb[i % 2].n
                for k in range(16):
                    sc.op("tensor",
                          lambda e, i=i, k=k: e.matmul(
                              pB[:, 0:512],
                              o4fm[:, i * D + k * HD:i * D + (k + 1) * HD],
                              wob[:, k * 512:(k + 1) * 512],
                              start=(k == 0), stop=(k == 15)),
                          waits=([(c_wo, wrdy), (c_dve, c_dve.n)]
                                 if k == 0 else []),
                          inc=c_pe)
                sc.op("vector",
                      lambda e, i=i, nb=nb, bx=bx: e.tensor_tensor(
                          x1pre[:, i * D + nb * 512:i * D + (nb + 1) * 512],
                          pB[:, 0:512], bx[:, :], ALU.add),
                      waits=[c_pe, (c_xb[i % 2], xrdy)], inc=c_dve)

        st_marks = {}
        for i in range(NCH):
            sl = x1pre[:, i * D:(i + 1) * D]
            wts = [(c_st, st_marks[i - 1])] if i >= 1 else []
            sc.op("vector",
                  lambda e, sl=sl: e.reduce_sum(scol[:, :], sl, axis=AX.X),
                  waits=[(c_dve, c_dve.n)] + wts, inc=c_dve)
            sc.op("vector",
                  lambda e: e.tensor_scalar(mcol[:, :], scol[:, :], 1.0 / D,
                                            None, ALU.mult),
                  inc=c_dve)
            sc.op("scalar",
                  lambda e, sl=sl: e.activation(sq[:, :], sl, AF.Square),
                  waits=[(c_dve, c_dve.n)], inc=c_act)
            sc.op("vector",
                  lambda e: e.reduce_sum(s2col[:, :], sq[:, :], axis=AX.X),
                  waits=[c_act], inc=c_dve)
            sc.op("vector",
                  lambda e: e.tensor_tensor(m2col[:, :], mcol[:, :],
                                            mcol[:, :], ALU.mult),
                  inc=c_dve)
            sc.op("vector",
                  lambda e: e.scalar_tensor_tensor(vcol[:, :], s2col[:, :],
                                                   1.0 / D, m2col[:, :],
                                                   ALU.mult, ALU.subtract),
                  inc=c_dve)
            sc.op("scalar",
                  lambda e: e.activation(sdl[:, :], vcol[:, :], AF.Sqrt,
                                         bias=epsc[:, 0:1]),
                  waits=[c_dve], inc=c_act)
            sc.op("vector", lambda e: e.reciprocal(rsl[:, :], sdl[:, :]),
                  waits=[c_act], inc=c_dve)
            sc.op("vector",
                  lambda e, sl=sl: e.tensor_scalar(
                      x1o[:, :], sl, mcol[:, 0:1], rsl[:, 0:1],
                      ALU.subtract, ALU.mult),
                  inc=c_dve)
            sc.dma("sync", x1_d[i * CH:(i + 1) * CH, :], x1o[:, :],
                   waits=[c_dve], inc=c_st)
            st_marks[i] = c_st.n
        sc.wait("sync", [(c_st, c_st.n)])
        sc.finalize(block)
    return nc



FWHT_SCALE = float(1.0 / np.sqrt(8192.0))


def build_launch_B2():
    nc = bass.Bass("TRN2", target_bir_lowering=False)
    x1t_d = nc.dram_tensor("x1t", [SPAN, D], F32, kind="ExternalInput").ap()
    x1fm_d = nc.dram_tensor("x1fm", [D, SPAN], F32, kind="ExternalInput").ap()
    w13_d = nc.dram_tensor("W13p", [64 * 128, 2 * D], F32, kind="ExternalInput").ap()
    w2t_d = nc.dram_tensor("W2T", [8192, D], F32, kind="ExternalInput").ap()
    h128_d = nc.dram_tensor("H128b", [128, 128], BF16, kind="ExternalInput").ap()
    hh_d = nc.dram_tensor("HHb", [128, 128], BF16, kind="ExternalInput").ap()
    x2_d = nc.dram_tensor("x2", [SPAN, D], F32, kind="ExternalOutput").ap()
    w2pp_d = nc.dram_tensor("w2pp", [4 * 64 * 128, 512], BF16, kind="Internal").ap()

    with ExitStack() as st:
        sb = lambda n, s, d: st.enter_context(nc.sbuf_tensor(n, s, d))
        psf = lambda n, w=512: st.enter_context(nc.psum_tensor(n, [128, w], F32))
        x1fm = sb("x1fm_s", [128, 16 * SPAN], BF16)
        S1buf = sb("S1buf", [128, 64 * 512], BF16)
        x2pre = sb("x2pre", [128, 4 * D], BF16)
        w13b = [sb(f"w13b{i}", [128, 2 * 16 * 128], BF16) for i in range(2)]
        w2g = [sb(f"w2g{i}", [128, D], BF16) for i in range(2)]
        w2o = sb("w2o", [128, D], BF16)
        wt = [sb(f"wt{i}", [128, 8 * 512], BF16) for i in range(2)]
        x1tb = [sb(f"x1tb{i}", [128, 512], F32) for i in range(2)]
        t1 = [sb(f"t1_{i}", [128, 512], BF16) for i in range(2)]
        hid = [sb(f"hid_{i}", [128, 512], BF16) for i in range(2)]
        h128 = sb("h128_s", [128, 128], BF16)
        hh = sb("hh_s", [128, 128], BF16)
        sqB = sb("sqB", [128, D], F32)
        x2o = sb("x2o", [128, D], F32)
        scol = sb("scolB", [128, 1], F32)
        mcol = sb("mcolB", [128, 1], F32)
        s2col = sb("s2colB", [128, 1], F32)
        m2col = sb("m2colB", [128, 1], F32)
        vcol = sb("vcolB", [128, 1], F32)
        sdl = sb("sdlB", [128, 1], F32)
        rsl = sb("rslB", [128, 1], F32)
        epsc = sb("epscB", [128, 1], F32)

        poolA = st.enter_context(nc.psum_tensor("poolA", [128, 2048], F32))
        poolB = st.enter_context(nc.psum_tensor("poolB", [128, 2048], F32))
        pH1 = [poolA[:, 0:512], poolA[:, 512:1024]]
        pH3 = [poolA[:, 1024:1536], poolA[:, 1536:2048]]
        pS1 = [poolB[:, 0:512], poolB[:, 512:1024]]
        pW = poolB[:, 1024:1536]
        pD = [poolA[:, i * 512:(i + 1) * 512] for i in range(4)]

        sc = Sched(nc, st)
        c_ld = sc.sem("c_ld")
        c_x = sc.sem("c_x")
        c_w1 = [sc.sem("c_w10"), sc.sem("c_w11")]
        c_wg = [sc.sem("c_wgA"), sc.sem("c_wgB")]
        c_wg2 = [sc.sem("c_wg2A"), sc.sem("c_wg2B")]
        c_wpp = sc.sem("c_wpp")
        c_wt = [sc.sem("c_wt0"), sc.sem("c_wt1")]
        c_xt = [sc.sem("c_xt0"), sc.sem("c_xt1")]
        c_pe = sc.sem("c_pe")
        c_dve = sc.sem("c_dve")
        c_act = sc.sem("c_act")
        c_st = sc.sem("c_st")
        block = st.enter_context(nc.Block())

        sc.dma("sync", h128[:, :], h128_d[:, :], inc=c_ld)
        sc.dma("sync", hh[:, :], hh_d[:, :], inc=c_ld)
        lds = c_ld.n
        sc.dma("gpsimd", *fm_pair(x1fm, x1fm_d), inc=c_x)
        x_rdy = c_x.n
        sc.op("vector", lambda e: e.memset(epsc[:, :], EPS), inc=c_dve)

        w2t_v = w2t_d.rearrange("(j r) d -> r j d", r=128)
        w2pp_v = w2pp_d.rearrange("(q j r) c -> q j r c", q=4, j=64)

        bmarks = {}

        def emit_build(ap2):
            a0, a1 = 2 * ap2, 2 * ap2 + 1
            gb = w2g[ap2 % 2]
            wts = [(c_pe, bmarks[ap2 - 2])] if ap2 >= 2 else []
            sc.dma("gpsimd", gb[0:64, :], w2t_v[a0], waits=wts,
                   inc=c_wg[ap2 % 2])
            sc.dma("gpsimd", gb[64:128, :], w2t_v[a1], inc=c_wg2[ap2 % 2])
            grdy = (c_wg[ap2 % 2], c_wg[ap2 % 2].n,
                    c_wg2[ap2 % 2], c_wg2[ap2 % 2].n)
            for nb in range(4):
                sc.op("tensor",
                      lambda e, nb=nb, gb=gb: e.matmul(
                          pW, hh[:, :], gb[:, nb * 512:(nb + 1) * 512],
                          start=True, stop=True),
                      waits=([(grdy[0], grdy[1]), (grdy[2], grdy[3]),
                              (c_ld, lds), (c_dve, c_dve.n)]
                             if nb == 0 else [(c_dve, c_dve.n)]),
                      inc=c_pe)
                if nb == 3:
                    bmarks[ap2] = c_pe.n
                sc.op("vector",
                      lambda e, nb=nb: e.tensor_copy(
                          w2o[:, nb * 512:(nb + 1) * 512], pW),
                      waits=[c_pe] + ([(c_wpp, c_wpp.n)] if nb == 0 else []),
                      inc=c_dve)
            for q in range(4):
                sc.dma("scalar", w2pp_v[q, :, a0, :],
                       w2o[0:64, q * 512:(q + 1) * 512],
                       waits=([(c_dve, c_dve.n)] if q == 0 else []),
                       inc=c_wpp)
                sc.dma("scalar", w2pp_v[q, :, a1, :],
                       w2o[64:128, q * 512:(q + 1) * 512], inc=c_wpp)

        act_marks = {}
        dve_marks = {}
        pe_marks = {}
        ct_marks = {}
        for th in range(2):
            for j in range(64):
                b13 = w13b[j % 2]
                gi = th * 64 + j
                wts = [(c_pe, pe_marks[gi - 2])] if gi >= 2 else []
                sc.dma("gpsimd", b13[:, :], w13_d[j * 128:(j + 1) * 128, :],
                       waits=wts, inc=c_w1[j % 2])
                w1rdy = c_w1[j % 2].n
                pq = j % 2
                w_h1 = [(c_w1[j % 2], w1rdy), (c_x, x_rdy)]
                b1 = b13
                if j >= 2:
                    w_h1 += [(c_act, act_marks[(th * 64 + j) - 2][0]),
                             (c_dve, dve_marks[(th * 64 + j) - 2])]
                else:
                    w_h1 += [(c_dve, c_dve.n), (c_act, c_act.n)]
                for k in range(16):
                    sc.op("tensor",
                          lambda e, k=k, th=th, b1=b1, pq=pq: e.matmul(
                              pH1[pq], b1[:, k * 128:(k + 1) * 128],
                              x1fm[:, k * SPAN + th * 512:k * SPAN + (th + 1) * 512],
                              start=(k == 0), stop=(k == 15)),
                          waits=(w_h1 if k == 0 else []),
                          inc=c_pe)
                sc.op("scalar",
                      lambda e, pq=pq: e.activation(t1[pq][:, :], pH1[pq],
                                                    AF.Silu),
                      waits=[c_pe], inc=c_act)
                silu_mark = c_act.n
                for k in range(16):
                    sc.op("tensor",
                          lambda e, k=k, th=th, b13=b13, pq=pq: e.matmul(
                              pH3[pq], b13[:, 2048 + k * 128:2048 + (k + 1) * 128],
                              x1fm[:, k * SPAN + th * 512:k * SPAN + (th + 1) * 512],
                              start=(k == 0), stop=(k == 15)),
                          waits=[], inc=c_pe)
                sc.op("vector",
                      lambda e, pq=pq: e.tensor_tensor(hid[pq][:, :],
                                                       t1[pq][:, :],
                                                       pH3[pq], ALU.mult),
                      waits=[c_pe, (c_act, silu_mark)], inc=c_dve)
                hid_mark = c_dve.n
                sc.op("tensor",
                      lambda e, pq=pq: e.matmul(pS1[pq], h128[:, :],
                                                hid[pq][:, :],
                                                start=True, stop=True),
                      waits=[(c_dve, hid_mark), (c_ld, lds)],
                      inc=c_pe)
                sc.op("scalar",
                      lambda e, j=j, pq=pq: e.activation(
                          S1buf[:, j * 512:(j + 1) * 512], pS1[pq], AF.Copy,
                          scale=FWHT_SCALE),
                      waits=[c_pe], inc=c_act)
                act_marks[th * 64 + j] = (c_act.n, silu_mark)
                dve_marks[th * 64 + j] = hid_mark
                if th == 0:
                    emit_build(j)
                pe_marks[th * 64 + j] = c_pe.n
            s1_rdy = c_act.n
            wpp_done = c_wpp.n
            for dnb in range(4):
                for jg in range(8):
                    buf = wt[jg % 2]
                    ci = th * 32 + dnb * 8 + jg
                    wts = [(c_pe, ct_marks[ci - 2])] if (ci >= 2 and
                                                        (dnb * 8 + jg) >= 2) \
                        else [(c_wpp, wpp_done), (c_pe, c_pe.n)]
                    sc.dma("sync", buf[:, :].rearrange("p (g c) -> p g c", g=8),
                           w2pp_v[dnb, jg * 8:(jg + 1) * 8, :, :]
                           .transpose([1, 0, 2]),
                           waits=wts, inc=c_wt[jg % 2])
                    wrdy = c_wt[jg % 2].n
                    for g in range(8):
                        jp = jg * 8 + g
                        for tm in range(4):
                            sc.op("tensor",
                                  lambda e, jp=jp, tm=tm, g=g, buf=buf:
                                  e.matmul(
                                      pD[tm],
                                      S1buf[:, jp * 512 + tm * 128:jp * 512 + (tm + 1) * 128],
                                      buf[:, g * 512:(g + 1) * 512],
                                      start=(jp == 0), stop=(jp == 63)),
                                  waits=([(c_wt[jg % 2], wrdy),
                                          (c_act, s1_rdy), (c_dve, c_dve.n)]
                                         if (g == 0 and tm == 0) else []),
                                  inc=c_pe)
                    ct_marks[th * 32 + dnb * 8 + jg] = c_pe.n
                mmdone = c_pe.n
                for tm in range(4):
                    bx = x1tb[tm % 2]
                    wts2 = [(c_dve, c_dve.n)] if (dnb * 4 + tm) >= 2 else []
                    row0 = (th * 4 + tm) * CH
                    sc.dma("sync", bx[:, :],
                           x1t_d[row0:row0 + CH, dnb * 512:(dnb + 1) * 512],
                           waits=wts2, inc=c_xt[tm % 2])
                    xrdy = c_xt[tm % 2].n
                    sc.op("vector",
                          lambda e, tm=tm, dnb=dnb, bx=bx: e.tensor_tensor(
                              x2pre[:, tm * D + dnb * 512:tm * D + (dnb + 1) * 512],
                              pD[tm], bx[:, :], ALU.add),
                          waits=[(c_pe, mmdone), (c_xt[tm % 2], xrdy)],
                          inc=c_dve)
            for tm in range(4):
                sl = x2pre[:, tm * D:(tm + 1) * D]
                sc.op("vector",
                      lambda e, sl=sl: e.reduce_sum(scol[:, :], sl, axis=AX.X),
                      waits=[(c_dve, c_dve.n), (c_st, c_st.n)], inc=c_dve)
                sc.op("vector",
                      lambda e: e.tensor_scalar(mcol[:, :], scol[:, :],
                                                1.0 / D, None, ALU.mult),
                      inc=c_dve)
                sc.op("scalar",
                      lambda e, sl=sl: e.activation(sqB[:, :], sl, AF.Square),
                      waits=[(c_dve, c_dve.n)], inc=c_act)
                sc.op("vector",
                      lambda e: e.reduce_sum(s2col[:, :], sqB[:, :], axis=AX.X),
                      waits=[c_act], inc=c_dve)
                sc.op("vector",
                      lambda e: e.tensor_tensor(m2col[:, :], mcol[:, :],
                                                mcol[:, :], ALU.mult),
                      inc=c_dve)
                sc.op("vector",
                      lambda e: e.scalar_tensor_tensor(vcol[:, :], s2col[:, :],
                                                       1.0 / D, m2col[:, :],
                                                       ALU.mult, ALU.subtract),
                      inc=c_dve)
                sc.op("scalar",
                      lambda e: e.activation(sdl[:, :], vcol[:, :], AF.Sqrt,
                                             bias=epsc[:, 0:1]),
                      waits=[c_dve], inc=c_act)
                sc.op("vector", lambda e: e.reciprocal(rsl[:, :], sdl[:, :]),
                      waits=[c_act], inc=c_dve)
                sc.op("vector",
                      lambda e, sl=sl: e.tensor_scalar(
                          x2o[:, :], sl, mcol[:, 0:1], rsl[:, 0:1],
                          ALU.subtract, ALU.mult),
                      inc=c_dve)
                row0 = (th * 4 + tm) * CH
                sc.dma("sync", x2_d[row0:row0 + CH, :], x2o[:, :],
                       waits=[c_dve], inc=c_st)
        sc.wait("sync", [(c_st, c_st.n)])
        sc.finalize(block)
    return nc



def build_launch_C():
    nc = bass.Bass("TRN2", target_bir_lowering=False)
    x2fm_d = nc.dram_tensor("x2fm", [D, SPAN], F32, kind="ExternalInput").ap()
    wbc_d = nc.dram_tensor("Wbcp", [128, 16 * 128], F32, kind="ExternalInput").ap()
    wdt_d = nc.dram_tensor("WdtT", [DTR, D], F32, kind="ExternalInput").ap()
    dt_d = nc.dram_tensor("dt_o", [D, SPAN], BF16, kind="ExternalOutput").ap()
    bc_d = nc.dram_tensor("bc_o", [64, SPAN], F32, kind="ExternalOutput").ap()

    with ExitStack() as st:
        sb = lambda n, s, d: st.enter_context(nc.sbuf_tensor(n, s, d))
        psf = lambda n, w=512: st.enter_context(nc.psum_tensor(n, [128, w], F32))
        x2fm = sb("x2fm_s", [128, 16 * SPAN], BF16)
        wbcs = sb("wbcs", [128, 16 * 128], BF16)
        wdts = sb("wdts", [DTR, D], BF16)
        dbc = sb("dbc", [128, SPAN], F32)
        dtrb = sb("dtrb", [DTR, SPAN], BF16)
        dts = sb("dts", [128, SPAN], BF16)
        pA = psf("pA")
        pB2 = psf("pB2")

        sc = Sched(nc, st)
        c_x = sc.sem("c_x")
        c_w = sc.sem("c_w")
        c_pe = sc.sem("c_pe")
        c_dve = sc.sem("c_dve")
        c_st = sc.sem("c_st")
        block = st.enter_context(nc.Block())

        sc.dma("gpsimd", *fm_pair(x2fm, x2fm_d), inc=c_x)
        sc.dma("gpsimd", wbcs[:, :], wbc_d[:, :], inc=c_w)
        sc.dma("gpsimd", wdts[:, :], wdt_d[:, :], inc=c_w)
        xrdy, wrdy = c_x.n, c_w.n
        for nb2 in range(2):
            for k in range(16):
                sc.op("tensor",
                      lambda e, k=k, nb2=nb2: e.matmul(
                          pA[:, :], wbcs[:, k * 128:(k + 1) * 128],
                          x2fm[:, k * SPAN + nb2 * 512:k * SPAN + (nb2 + 1) * 512],
                          start=(k == 0), stop=(k == 15)),
                      waits=([(c_x, xrdy), (c_w, wrdy), (c_dve, c_dve.n)]
                             if k == 0 else []),
                      inc=c_pe)
            sc.op("vector",
                  lambda e, nb2=nb2: e.tensor_copy(
                      dbc[:, nb2 * 512:(nb2 + 1) * 512], pA[:, :]),
                  waits=[c_pe], inc=c_dve)
        sc.op("vector", lambda e: e.tensor_copy(dtrb[:, :], dbc[0:64, :]),
              inc=c_dve)
        dtr_rdy = c_dve.n
        sc.dma("sync", bc_d[:, :], dbc[64:128, :], waits=[c_dve], inc=c_st)
        st_marks = {}
        for m in range(16):
            for nb2 in range(2):
                sc.op("tensor",
                      lambda e, m=m, nb2=nb2: e.matmul(
                          pB2[:, :], wdts[:, m * 128:(m + 1) * 128],
                          dtrb[:, nb2 * 512:(nb2 + 1) * 512],
                          start=True, stop=True),
                      waits=[(c_dve, max(dtr_rdy, c_dve.n)), (c_w, wrdy)],
                      inc=c_pe)
                sc.op("vector",
                      lambda e, nb2=nb2: e.tensor_copy(
                          dts[:, nb2 * 512:(nb2 + 1) * 512], pB2[:, :]),
                      waits=[c_pe] + ([(c_st, st_marks[m - 1])]
                                      if (m >= 1 and nb2 == 0) else []),
                      inc=c_dve)
            sc.dma("sync", dt_d[m * 128:(m + 1) * 128, :], dts[:, :],
                   waits=[c_dve], inc=c_st)
            st_marks[m] = c_st.n
        sc.wait("sync", [(c_st, c_st.n)])
        sc.finalize(block)
    return nc



TH_D = 2048


def build_launch_D():
    nc = bass.Bass("TRN2", target_bir_lowering=False)
    dt_d = nc.dram_tensor("dt_in", [4 * 128, N], BF16, kind="ExternalInput").ap()
    u_d = nc.dram_tensor("u_in", [4 * 128, N], F32, kind="ExternalInput").ap()
    bc_d = nc.dram_tensor("bc_in", [128, N], F32, kind="ExternalInput").ap()
    al_d = nc.dram_tensor("alog_in", [128, 64], F32, kind="ExternalInput").ap()
    bdt_d = nc.dram_tensor("bdt_in", [128, 2], F32, kind="ExternalInput").ap()
    dD_d = nc.dram_tensor("dD_in", [128, 2], F32, kind="ExternalInput").ap()
    y_d = nc.dram_tensor("y_out", [4 * 128, N], BF16, kind="ExternalOutput").ap()

    with ExitStack() as st:
        sb = lambda n, s, d: st.enter_context(nc.sbuf_tensor(n, s, d))
        delta = [sb(f"delta{j}", [128, TH_D], BF16) for j in range(4)]
        wv = [sb(f"wv{j}", [128, TH_D], BF16) for j in range(4)]
        yv = [sb(f"yv{j}", [128, TH_D], F32) for j in range(4)]
        dtb = sb("dtb", [128, TH_D], BF16)
        ub = sb("ub", [128, TH_D], BF16)
        s0b = sb("s0b", [128, TH_D], F32)
        a1b = sb("a1b", [128, TH_D], F32)
        l1b = sb("l1b", [128, TH_D], F32)
        bcbf = sb("bcbf", [128, N], BF16)
        NPAR = 3
        Brep = [[sb(f"Brep{b}_{p}", [128, TH_D], BF16) for p in range(NPAR)]
                for b in range(2)]
        Crep = [[sb(f"Crep{b}_{p}", [128, TH_D], BF16) for p in range(NPAR)]
                for b in range(2)]
        dAf = [sb(f"dAf{i}", [128, TH_D], F32) for i in range(2)]
        dBu = [sb(f"dBu{i}", [128, TH_D], BF16) for i in range(2)]
        hv = [sb(f"hv{i}", [128, TH_D], BF16) for i in range(2)]
        tmpv = [sb(f"tmpv{i}", [128, TH_D], BF16) for i in range(2)]
        carry = [sb(f"carry{j}", [128, 32], BF16) for j in range(4)]
        aexp = sb("aexp", [128, 64], F32)
        acol = sb("acol", [128, 64], F32)
        alog = sb("alog_s", [128, 64], F32)
        bdt = sb("bdt_s", [128, 2], F32)
        dD = sb("dD_s", [128, 2], F32)
        onec = sb("onec", [128, 1], F32)

        sc = Sched(nc, st)
        c_ld = sc.sem("c_ld")
        c_t = [sc.sem(f"c_t{j}") for j in range(2)]
        c_bcB = [[sc.sem(f"c_bcB{q}_{p}") for p in range(3)] for q in range(2)]
        c_bcC = [[sc.sem(f"c_bcC{q}_{p}") for p in range(3)] for q in range(2)]
        c_dve = sc.sem("c_dve")
        c_act = sc.sem("c_act")
        c_gp = sc.sem("c_gp")
        c_st = sc.sem("c_st")
        block = st.enter_context(nc.Block())

        sc.dma("sync", alog[:, :], al_d[:, :], inc=c_ld)
        sc.dma("sync", bdt[:, :], bdt_d[:, :], inc=c_ld)
        sc.dma("sync", dD[:, :], dD_d[:, :], inc=c_ld)
        lds = c_ld.n
        c_bcl = sc.sem("c_bcl")
        sc.dma("gpsimd", bcbf[:, :], bc_d[:, :], inc=c_bcl)
        bcl = c_bcl.n
        sc.op("vector", lambda e: e.memset(onec[:, :], 1.0), inc=c_dve)
        sc.op("scalar", lambda e: e.activation(aexp[:, :], alog[:, :], AF.Exp),
              waits=[(c_ld, lds)], inc=c_act)
        sc.op("vector", lambda e: e.tensor_scalar(acol[:, :], aexp[:, :], -1.0,
                                                  None, ALU.mult),
              waits=[c_act], inc=c_dve)
        acol_rdy = c_dve.n


        ysts = []
        for th in range(2):
            t0 = th * TH_D
            for j in range(4):
                blk = j % 2
                wts = [(c_dve, c_dve.n)]
                sc.dma("sync", dtb[:, :],
                       dt_d[j * 128:(j + 1) * 128, t0:t0 + TH_D],
                       waits=wts, inc=c_t[0])
                sc.dma("gpsimd", ub[:, :],
                       u_d[j * 128:(j + 1) * 128, t0:t0 + TH_D],
                       waits=wts, inc=c_t[1])
                trdy0, trdy1 = c_t[0].n, c_t[1].n
                sc.op("vector",
                      lambda e, blk=blk: e.tensor_scalar(
                          s0b[:, :], dtb[:, :], bdt[:, blk:blk + 1], None,
                          ALU.add),
                      waits=[(c_t[0], trdy0), (c_ld, lds)], inc=c_dve)
                sc.op("scalar", lambda e: e.activation(a1b[:, :], s0b[:, :],
                                                       AF.Abs),
                      waits=[c_dve], inc=c_act)
                sc.op("scalar", lambda e: e.activation(a1b[:, :], a1b[:, :],
                                                       AF.Exp, scale=-1.0),
                      inc=c_act)
                sc.op("scalar",
                      lambda e: e.activation(l1b[:, :], a1b[:, :], AF.Ln,
                                             bias=onec[:, 0:1]),
                      waits=[(c_dve, c_dve.n)], inc=c_act)
                sc.op("vector",
                      lambda e: e.tensor_scalar(s0b[:, :], s0b[:, :], 0.0,
                                                None, ALU.max),
                      waits=[(c_act, c_act.n)], inc=c_dve)
                sc.op("vector",
                      lambda e, j=j: e.tensor_tensor(delta[j][:, :], s0b[:, :],
                                                     l1b[:, :], ALU.add),
                      waits=[c_act], inc=c_dve)
                sc.op("vector",
                      lambda e, j=j: e.tensor_tensor(wv[j][:, :],
                                                     delta[j][:, :], ub[:, :],
                                                     ALU.mult),
                      waits=[(c_t[1], trdy1)], inc=c_dve)
                sc.op("vector",
                      lambda e, j=j, blk=blk: e.tensor_scalar(
                          yv[j][:, :], ub[:, :], dD[:, blk:blk + 1], None,
                          ALU.mult),
                      waits=[(c_st, c_st.n)], inc=c_dve)
            def emit_chains(n):
                p = n % NPAR
                for b in range(2):
                    eng = "sync"
                    csB, csC = c_bcB[b][p], c_bcC[b][p]
                    bB, bC = Brep[b][p], Crep[b][p]
                    sc.dma(eng, bB[0:1, :],
                           bcbf[b * 64 + n:b * 64 + n + 1, t0:t0 + TH_D],
                           waits=[(c_dve, c_dve.n), (csB, csB.n),
                                  (c_bcl, bcl)], inc=csB)
                    sc.dma(eng, bC[0:1, :],
                           bcbf[b * 64 + 32 + n:b * 64 + 33 + n, t0:t0 + TH_D],
                           waits=[(csC, csC.n)], inc=csC)
                    nn = 1
                    while nn < 128:
                        sc.dma(eng, bB[nn:2 * nn, :], bB[0:nn, :],
                               waits=[(csB, csB.n)], inc=csB)
                        sc.dma(eng, bC[nn:2 * nn, :], bC[0:nn, :],
                               waits=[(csC, csC.n)], inc=csC)
                        nn *= 2
                return ((c_bcB[0][p], c_bcB[0][p].n),
                        (c_bcC[0][p], c_bcC[0][p].n),
                        (c_bcB[1][p], c_bcB[1][p].n),
                        (c_bcC[1][p], c_bcC[1][p].n))

            rdy = {}
            scan_marks = {}
            for n in range(32):
                p = n % NPAR
                if n == 0:
                    rdy[0] = emit_chains(0)
                    rdy[1] = emit_chains(1)
                    rdy[2] = emit_chains(2)
                elif n + 2 < 32:
                    rdy[n + 2] = emit_chains(n + 2)
                rep_rdy0 = rdy[n][0]
                rep_rdy0C = rdy[n][1]
                rep_rdy1 = rdy[n][2]
                rep_rdy1C = rdy[n][3]
                for j in range(4):
                    blk, b = j % 2, j // 2
                    it = n * 4 + j
                    q = it % 2
                    w_act = [(c_dve, acol_rdy)]
                    if it >= 2:
                        w_act.append(scan_marks[it - 2])
                    else:
                        w_act.append((c_dve, c_dve.n))
                    sc.op("scalar",
                          lambda e, j=j, blk=blk, n=n, q=q: e.activation(
                              dAf[q][:, :], delta[j][:, :], AF.Exp,
                              scale=acol[:, blk * 32 + n:blk * 32 + n + 1]),
                          waits=w_act, inc=c_act)
                    act_mark = c_act.n
                    sc.op("vector",
                          lambda e, j=j, b=b, p=p, q=q: e.tensor_tensor(
                              dBu[q][:, :], wv[j][:, :], Brep[b][p][:, :],
                              ALU.mult),
                          waits=[rep_rdy0 if b == 0 else rep_rdy1],
                          inc=c_dve)
                    seng = "vector"
                    sctr = c_dve
                    dbu_mark = (c_dve, c_dve.n)
                    if th == 0:
                        sc.op(seng,
                              lambda e, q=q: e.tensor_tensor_scan(
                                  hv[q][:, :], dAf[q][:, :], dBu[q][:, :],
                                  0.0, ALU.mult, ALU.add),
                              waits=[(c_act, act_mark), dbu_mark], inc=sctr)
                        scan_marks[it] = (sctr, sctr.n)
                        sc.op("vector",
                              lambda e, j=j, n=n, q=q: e.tensor_copy(
                                  carry[j][:, n:n + 1],
                                  hv[q][:, TH_D - 1:TH_D]),
                              waits=[scan_marks[it]], inc=c_dve)
                    else:
                        sc.op(seng,
                              lambda e, j=j, n=n, q=q: e.tensor_tensor_scan(
                                  hv[q][:, :], dAf[q][:, :], dBu[q][:, :],
                                  carry[j][:, n:n + 1], ALU.mult, ALU.add),
                              waits=[(c_act, act_mark), dbu_mark], inc=sctr)
                        scan_marks[it] = (sctr, sctr.n)
                    sc.op("vector",
                          lambda e, b=b, p=p, j=j, q=q: e.tensor_tensor(
                              tmpv[j % 2][:, :], hv[q][:, :], Crep[b][p][:, :],
                              ALU.mult),
                          waits=[rep_rdy0C if b == 0 else rep_rdy1C,
                                 (c_gp, c_gp.n), scan_marks[it]],
                          inc=c_dve)
                    sc.op("gpsimd",
                          lambda e, j=j: e.tensor_tensor(
                              yv[j][:, :], yv[j][:, :], tmpv[j % 2][:, :],
                              ALU.add),
                          waits=[(c_dve, c_dve.n)], inc=c_gp)
            for j in range(4):
                sc.dma("gpsimd", y_d[j * 128:(j + 1) * 128, t0:t0 + TH_D],
                       yv[j][:, :],
                       waits=[(c_gp, c_gp.n)] if j == 0 else [], inc=c_st)
            ysts.append(c_st.n)
        sc.wait("sync", [(c_st, c_st.n)])
        sc.finalize(block)
    return nc



def build_launch_E():
    nc = bass.Bass("TRN2", target_bir_lowering=False)
    x2_d = nc.dram_tensor("x2t", [SPAN, D], F32, kind="ExternalInput").ap()
    y_d = nc.dram_tensor("y_tm", [SPAN, D], BF16, kind="ExternalInput").ap()
    wgu2_d = nc.dram_tensor("Wgup", [64 * 128, 2 * D], F32, kind="ExternalInput").ap()
    wdn_d = nc.dram_tensor("Wdnp", [4 * 64 * 128, 512], F32, kind="ExternalInput").ap()
    ident_d = nc.dram_tensor("ident_b", [CH, CH], BF16, kind="ExternalInput").ap()
    out_d = nc.dram_tensor("out_o", [SPAN, D], F32, kind="ExternalOutput").ap()

    with ExitStack() as st:
        sb = lambda n, s, d: st.enter_context(nc.sbuf_tensor(n, s, d))
        psf = lambda n, w=512: st.enter_context(nc.psum_tensor(n, [128, w], F32))
        x3buf = sb("x3buf", [128, NCH * D], BF16)
        x3fm = sb("x3fm", [128, 16 * SPAN], BF16)
        innerb = sb("innerb", [128, 64 * 512], BF16)
        x4pre = sb("x4pre", [128, 4 * D], BF16)
        x2b = [sb(f"x2b{i}", [128, D], F32) for i in range(2)]
        yb = [sb(f"yb{i}", [128, D], BF16) for i in range(2)]
        x3pre = sb("x3pre", [128, D], F32)
        wgub = [sb(f"wgub{i}", [128, 2 * 16 * 128], BF16) for i in range(2)]
        wtE = [sb(f"wtE{i}", [128, 4 * 512], BF16) for i in range(2)]
        t1E = [sb(f"t1E{i}", [128, 512], BF16) for i in range(2)]
        sqE = sb("sqE", [128, D], BF16)
        ident = sb("identE", [CH, CH], BF16)
        scol = sb("scolE", [128, 1], F32)
        mcol = sb("mcolE", [128, 1], F32)
        s2col = sb("s2colE", [128, 1], F32)
        m2col = sb("m2colE", [128, 1], F32)
        vcol = sb("vcolE", [128, 1], F32)
        sdl = sb("sdlE", [128, 1], F32)
        rsl = sb("rslE", [128, 1], F32)
        epsc = sb("epscE", [128, 1], F32)

        poolE = st.enter_context(nc.psum_tensor("poolE", [128, 2048], F32))
        pG = [poolE[:, 0:512], poolE[:, 512:1024]]
        pU = [poolE[:, 1024:1536], poolE[:, 1536:2048]]
        pDl = [poolE[:, i * 512:(i + 1) * 512] for i in range(4)]
        pT = st.enter_context(nc.psum_tensor("pTE", [128, 512], BF16))

        sc = Sched(nc, st)
        c_ld = sc.sem("c_ld")
        c_x2 = [sc.sem("c_x2a"), sc.sem("c_x2b")]
        c_y = [sc.sem("c_ya"), sc.sem("c_yb")]
        c_wg = [sc.sem("c_wga"), sc.sem("c_wgb")]
        c_wu = [sc.sem("c_wua"), sc.sem("c_wub")]
        c_wt = [sc.sem("c_wta"), sc.sem("c_wtb")]
        c_pe = sc.sem("c_pe")
        c_dve = sc.sem("c_dve")
        c_act = sc.sem("c_act")
        c_st = sc.sem("c_st")
        block = st.enter_context(nc.Block())

        sc.dma("sync", ident[:, :], ident_d[:, :], inc=c_ld)
        lds = c_ld.n
        sc.op("vector", lambda e: e.memset(epsc[:, :], EPS), inc=c_dve)

        for i in range(NCH):
            bx, by = x2b[i % 2], yb[i % 2]
            wts = [(c_dve, c_dve.n)] if i >= 2 else []
            sc.dma("sync", bx[:, :], x2_d[i * CH:(i + 1) * CH, :], waits=wts,
                   inc=c_x2[i % 2])
            sc.dma("sync", by[:, :], y_d[i * CH:(i + 1) * CH, :],
                   inc=c_y[i % 2])
            xrdy, yrdy = c_x2[i % 2].n, c_y[i % 2].n
            sc.op("vector",
                  lambda e, bx=bx, by=by: e.tensor_tensor(
                      x3pre[:, :], bx[:, :], by[:, :], ALU.add),
                  waits=[(c_x2[i % 2], xrdy), (c_y[i % 2], yrdy)], inc=c_dve)
            sc.op("vector",
                  lambda e: e.reduce_sum(scol[:, :], x3pre[:, :], axis=AX.X),
                  inc=c_dve)
            sc.op("vector",
                  lambda e: e.tensor_scalar(mcol[:, :], scol[:, :], 1.0 / D,
                                            None, ALU.mult),
                  inc=c_dve)
            sc.op("scalar",
                  lambda e: e.activation(sqE[:, :], x3pre[:, :], AF.Square),
                  waits=[(c_dve, c_dve.n)], inc=c_act)
            sc.op("vector",
                  lambda e: e.reduce_sum(s2col[:, :], sqE[:, :], axis=AX.X),
                  waits=[c_act], inc=c_dve)
            sc.op("vector",
                  lambda e: e.tensor_tensor(m2col[:, :], mcol[:, :],
                                            mcol[:, :], ALU.mult),
                  inc=c_dve)
            sc.op("vector",
                  lambda e: e.scalar_tensor_tensor(vcol[:, :], s2col[:, :],
                                                   1.0 / D, m2col[:, :],
                                                   ALU.mult, ALU.subtract),
                  inc=c_dve)
            sc.op("scalar",
                  lambda e: e.activation(sdl[:, :], vcol[:, :], AF.Sqrt,
                                         bias=epsc[:, 0:1]),
                  waits=[c_dve], inc=c_act)
            sc.op("vector", lambda e: e.reciprocal(rsl[:, :], sdl[:, :]),
                  waits=[c_act], inc=c_dve)
            sc.op("vector",
                  lambda e, i=i: e.tensor_scalar(
                      x3buf[:, i * D:(i + 1) * D], x3pre[:, :], mcol[:, 0:1],
                      rsl[:, 0:1], ALU.subtract, ALU.mult),
                  inc=c_dve)
            for k in range(16):
                sc.op("tensor",
                      lambda e, i=i, k=k: e.transpose(
                          pT[:, 0:CH], x3buf[:, i * D + k * HD:i * D + (k + 1) * HD],
                          ident[:, :]),
                      waits=[(c_dve, c_dve.n), (c_ld, lds)], inc=c_pe)
                sc.op("vector",
                      lambda e, i=i, k=k: e.tensor_copy(
                          x3fm[:, k * SPAN + i * CH:k * SPAN + (i + 1) * CH],
                          pT[:, 0:CH]),
                      waits=[c_pe], inc=c_dve)
        x3_rdy = c_dve.n

        act_me = {}
        dve_me = {}
        pe_me = {}
        ct_me = {}
        for th in range(2):
            for j in range(64):
                bgu = wgub[j % 2]
                pq = j % 2
                gi0 = th * 64 + j
                wts = [(c_pe, pe_me[gi0 - 2])] if gi0 >= 2 else []
                sc.dma("gpsimd", bgu[:, :], wgu2_d[j * 128:(j + 1) * 128, :],
                       waits=wts, inc=c_wg[j % 2])
                grdy = c_wg[j % 2].n
                w_g = [(c_wg[j % 2], grdy), (c_dve, x3_rdy)]
                bg = bgu
                if j >= 2:
                    w_g += [(c_act, act_me[th * 64 + j - 2][0]),
                            (c_dve, dve_me[th * 64 + j - 2])]
                else:
                    w_g += [(c_dve, c_dve.n), (c_act, c_act.n)]
                for k in range(16):
                    sc.op("tensor",
                          lambda e, k=k, th=th, bg=bg, pq=pq: e.matmul(
                              pG[pq], bg[:, k * 128:(k + 1) * 128],
                              x3fm[:, k * SPAN + th * 512:k * SPAN + (th + 1) * 512],
                              start=(k == 0), stop=(k == 15)),
                          waits=(w_g if k == 0 else []),
                          inc=c_pe)
                sc.op("scalar",
                      lambda e, pq=pq: e.activation(t1E[pq][:, :], pG[pq],
                                                    AF.Silu),
                      waits=[c_pe], inc=c_act)
                silu_me = c_act.n
                for k in range(16):
                    sc.op("tensor",
                          lambda e, k=k, th=th, bgu=bgu, pq=pq: e.matmul(
                              pU[pq], bgu[:, 2048 + k * 128:2048 + (k + 1) * 128],
                              x3fm[:, k * SPAN + th * 512:k * SPAN + (th + 1) * 512],
                              start=(k == 0), stop=(k == 15)),
                          waits=[], inc=c_pe)
                sc.op("vector",
                      lambda e, j=j, pq=pq: e.tensor_tensor(
                          innerb[:, j * 512:(j + 1) * 512], t1E[pq][:, :],
                          pU[pq], ALU.mult),
                      waits=[c_pe, (c_act, silu_me)], inc=c_dve)
                act_me[th * 64 + j] = (silu_me,)
                dve_me[th * 64 + j] = c_dve.n
                pe_me[th * 64 + j] = c_pe.n
            inner_rdy = c_dve.n
            wdn_v = wdn_d.rearrange("(q j r) c -> q j r c", q=4, j=64)
            for dnb in range(4):
                for jg in range(16):
                    buf = wtE[jg % 2]
                    ci = th * 64 + dnb * 16 + jg
                    wts = [(c_pe, ct_me[ci - 2])] if (ci >= 2 and
                                                     (dnb * 16 + jg) >= 2) \
                        else [(c_pe, c_pe.n)]
                    sc.dma("gpsimd",
                           buf[:, :].rearrange("p (g c) -> p g c", g=4),
                           wdn_v[dnb, jg * 4:(jg + 1) * 4, :, :]
                           .transpose([1, 0, 2]),
                           waits=wts, inc=c_wt[jg % 2])
                    wrdy = c_wt[jg % 2].n
                    for g in range(4):
                        jp = jg * 4 + g
                        for tm in range(4):
                            sc.op("tensor",
                                  lambda e, jp=jp, tm=tm, g=g, buf=buf:
                                  e.matmul(
                                      pDl[tm],
                                      innerb[:, jp * 512 + tm * 128:jp * 512 + (tm + 1) * 128],
                                      buf[:, g * 512:(g + 1) * 512],
                                      start=(jp == 0), stop=(jp == 63)),
                                  waits=([(c_wt[jg % 2], wrdy),
                                          (c_dve, max(inner_rdy, c_dve.n)),
                                          (c_act, c_act.n)]
                                         if (g == 0 and tm == 0) else []),
                                  inc=c_pe)
                    ct_me[th * 64 + dnb * 16 + jg] = c_pe.n
                mmdone = c_pe.n
                for tm in range(4):
                    gi = th * 4 + tm
                    sc.op("vector",
                          lambda e, tm=tm, dnb=dnb, gi=gi: e.tensor_tensor(
                              x4pre[:, tm * D + dnb * 512:tm * D + (dnb + 1) * 512],
                              pDl[tm],
                              x3buf[:, gi * D + dnb * 512:gi * D + (dnb + 1) * 512],
                              ALU.add),
                          waits=[(c_pe, mmdone)], inc=c_dve)
            for tm in range(4):
                sl = x4pre[:, tm * D:(tm + 1) * D]
                sc.op("vector",
                      lambda e, sl=sl: e.reduce_sum(scol[:, :], sl, axis=AX.X),
                      waits=[(c_dve, c_dve.n), (c_st, c_st.n)], inc=c_dve)
                sc.op("vector",
                      lambda e: e.tensor_scalar(mcol[:, :], scol[:, :],
                                                1.0 / D, None, ALU.mult),
                      inc=c_dve)
                sc.op("scalar",
                      lambda e, sl=sl: e.activation(sqE[:, :], sl, AF.Square),
                      waits=[(c_dve, c_dve.n)], inc=c_act)
                sc.op("vector",
                      lambda e: e.reduce_sum(s2col[:, :], sqE[:, :], axis=AX.X),
                      waits=[c_act], inc=c_dve)
                sc.op("vector",
                      lambda e: e.tensor_tensor(m2col[:, :], mcol[:, :],
                                                mcol[:, :], ALU.mult),
                      inc=c_dve)
                sc.op("vector",
                      lambda e: e.scalar_tensor_tensor(vcol[:, :], s2col[:, :],
                                                       1.0 / D, m2col[:, :],
                                                       ALU.mult, ALU.subtract),
                      inc=c_dve)
                sc.op("scalar",
                      lambda e: e.activation(sdl[:, :], vcol[:, :], AF.Sqrt,
                                             bias=epsc[:, 0:1]),
                      waits=[c_dve], inc=c_act)
                sc.op("vector", lambda e: e.reciprocal(rsl[:, :], sdl[:, :]),
                      waits=[c_act], inc=c_dve)
                sc.op("vector",
                      lambda e, sl=sl: e.tensor_scalar(
                          x3pre[:, :], sl, mcol[:, 0:1], rsl[:, 0:1],
                          ALU.subtract, ALU.mult),
                      inc=c_dve)
                row0 = (th * 4 + tm) * CH
                sc.dma("sync", out_d[row0:row0 + CH, :], x3pre[:, :],
                       waits=[c_dve], inc=c_st)
        sc.wait("sync", [(c_st, c_st.n)])
        sc.finalize(block)
    return nc


def retile_cols(wt, colblk):
    din, dout = wt.shape
    r = wt.reshape(din // 128, 128, dout // colblk, colblk)
    r = r.transpose(2, 1, 0, 3).reshape(dout // colblk, 128, -1)
    return np.ascontiguousarray(r.reshape(dout // colblk * 128, -1))


def hadamard(n):
    h = np.array([[1.0]])
    while h.shape[0] < n:
        h = np.block([[h, h], [h, -h]])
    return h



def host_consts_A():
    s = np.arange(CH, dtype=np.float32)
    tridT = s[None, :] - s[:, None]
    tridT = np.where(tridT >= 0, tridT, 1e9).astype(np.float32)
    return dict(
        idx1_row=(s + 1.0).astype(np.float32).reshape(1, CH),
        rev_col=(CH - 1.0 - s).astype(np.float32).reshape(CH, 1),
        tridT=tridT,
        ones_f=np.ones((1, CH), np.float32),
        ones_b=np.ones((1, CH), BF),
        ident_b=np.eye(CH, dtype=np.float32).astype(BF),
    )


_BUILD_CACHE = {}


def _get(name, builder):
    if name not in _BUILD_CACHE:
        _BUILD_CACHE[name] = builder()
    return _BUILD_CACHE[name]


def kernel(**inputs):
    x = np.asarray(inputs["L0_attn_x"], np.float32)
    ld = np.asarray(inputs["L0_attn_log_decay"], np.float32)
    Wq = np.asarray(inputs["L0_attn_Wq"], np.float32)
    Wk = np.asarray(inputs["L0_attn_Wk"], np.float32)
    Wv = np.asarray(inputs["L0_attn_Wv"], np.float32)
    Wo = np.asarray(inputs["L0_attn_Wo"], np.float32)
    Wg1 = np.asarray(inputs["L0_attn_Wg1"], np.float32)
    Wg2 = np.asarray(inputs["L0_attn_Wg2"], np.float32)
    norm_w = np.asarray(inputs["L0_attn_norm_w"], np.float32)
    W1 = np.asarray(inputs["L0_ffn_W1"], np.float32)
    W2 = np.asarray(inputs["L0_ffn_W2"], np.float32)
    W3 = np.asarray(inputs["L0_ffn_W3"], np.float32)
    Wbc = np.asarray(inputs["L1_attn_W_bc"], np.float32)
    Wdt = np.asarray(inputs["L1_attn_W_dt"], np.float32)
    bdt = np.asarray(inputs["L1_attn_b_dt"], np.float32)
    Alog = np.asarray(inputs["L1_attn_A_log"], np.float32)
    Dp = np.asarray(inputs["L1_attn_D_param"], np.float32)
    wgu = np.asarray(inputs["L1_ffn_w_gate_up"], np.float32)
    wdn = np.asarray(inputs["L1_ffn_w_down"], np.float32)

    cores = list(range(NCORE))
    spans = [(c // 4, (c % 4) * SPAN) for c in cores]
    consts = host_consts_A()
    ldr = np.ascontiguousarray(ld.reshape(1, H))
    xs = [np.ascontiguousarray(x[b, s:s + SPAN]) for b, s in spans]
    xfms = [np.ascontiguousarray(t.T) for t in xs]

    ncA = _get("A", build_launch_A)
    Wqkp = np.ascontiguousarray(np.concatenate(
        [retile_cols(Wq, 128).reshape(H, 128, D),
         retile_cols(Wk, 128).reshape(H, 128, D)], axis=2).reshape(H * 128, 2 * D))
    Wvp = retile_cols(Wv, 512)
    inA = [dict(consts, xfm=xfms[c], Wqkp=Wqkp, Wvp=Wvp, ld_row=ldr)
           for c in cores]
    rA = run_bass_kernel_spmd(ncA, inA, core_ids=cores).results
    s_all = np.ascontiguousarray(
        np.concatenate([rA[c]["s_local"] for c in cores], axis=0))

    kvecs = []
    for c in cores:
        kv = np.full((1, NCORE), 1e9, np.float32)
        for r in cores:
            if r // 4 == c // 4 and (r % 4) < (c % 4):
                kv[0, r] = SPAN * ((c % 4) - 1 - (r % 4))
        kvecs.append(kv)
    nwfm = np.ascontiguousarray(norm_w.reshape(16, 128).T)
    ncB1 = _get("B1", build_launch_B1)
    Wg1p = retile_cols(Wg1, 128)
    Wop = retile_cols(Wo, 512)
    inB1 = [dict(x_tm=xs[c], xfm=xfms[c], o_loc=rA[c]["o_loc"],
                 qs=rA[c]["qs"], s_all=s_all, kvec=kvecs[c], ld_row=ldr,
                 Wg1p=Wg1p, Wg2=Wg2, normw_fm=nwfm, Wop=Wop,
                 ones_f=consts["ones_f"], ident_b=consts["ident_b"])
            for c in cores]
    rB1 = run_bass_kernel_spmd(ncB1, inB1, core_ids=cores).results
    x1s = [np.asarray(rB1[c]["x1"]) for c in cores]

    W13p = np.ascontiguousarray(np.concatenate(
        [retile_cols(np.ascontiguousarray(W1.T), 128).reshape(64, 128, D),
         retile_cols(np.ascontiguousarray(W3.T), 128).reshape(64, 128, D)],
        axis=2).reshape(64 * 128, 2 * D))
    W2T = np.ascontiguousarray(W2.T)
    H128b = hadamard(128).astype(BF)
    HHb = np.kron(np.eye(2), hadamard(64)).astype(BF)
    ncB2 = _get("B2", build_launch_B2)
    inB2 = [dict(x1t=x1s[c], x1fm=np.ascontiguousarray(x1s[c].T),
                 W13p=W13p, W2T=W2T, H128b=H128b, HHb=HHb)
            for c in cores]
    rB2 = run_bass_kernel_spmd(ncB2, inB2, core_ids=cores).results
    x2s = [np.asarray(rB2[c]["x2"]) for c in cores]
    x2fmTs = [np.ascontiguousarray(x2s[c].T) for c in cores]

    Wbcp = retile_cols(np.ascontiguousarray(Wbc.T), 128)
    WdtT = np.ascontiguousarray(Wdt.T)
    ncC = _get("C", build_launch_C)
    inC = [dict(x2fm=x2fmTs[c], Wbcp=Wbcp, WdtT=WdtT) for c in cores]
    rC = run_bass_kernel_spmd(ncC, inC, core_ids=cores).results

    ncD = _get("D", build_launch_D)
    inD = []
    for kcore in cores:
        d2lo = kcore * D2
        dt_in = np.empty((512, N), BF)
        u_in = np.empty((512, N), np.float32)
        for j in range(4):
            b, blk = j // 2, j % 2
            rows = slice(d2lo + blk * 128, d2lo + (blk + 1) * 128)
            for s in range(4):
                src_c = 4 * b + s
                cols = slice(s * SPAN, (s + 1) * SPAN)
                dt_in[j * 128:(j + 1) * 128, cols] =                     np.asarray(rC[src_c]["dt_o"])[rows, :]
                u_in[j * 128:(j + 1) * 128, cols] = x2fmTs[src_c][rows, :]
        bc_in = np.empty((128, N), np.float32)
        for b in range(2):
            for s in range(4):
                src_c = 4 * b + s
                cols = slice(s * SPAN, (s + 1) * SPAN)
                bc_in[b * 64:(b + 1) * 64, cols] =                     np.asarray(rC[src_c]["bc_o"])
        alog_in = np.empty((128, 64), np.float32)
        for blk in range(2):
            alog_in[:, blk * 32:(blk + 1) * 32] =                 Alog[d2lo + blk * 128:d2lo + (blk + 1) * 128]
        bdt_in = np.stack([bdt[d2lo:d2lo + 128],
                           bdt[d2lo + 128:d2lo + 256]], axis=1)
        dD_in = np.stack([Dp[d2lo:d2lo + 128],
                          Dp[d2lo + 128:d2lo + 256]], axis=1)
        inD.append(dict(dt_in=dt_in, u_in=u_in, bc_in=bc_in,
                        alog_in=np.ascontiguousarray(alog_in),
                        bdt_in=np.ascontiguousarray(bdt_in),
                        dD_in=np.ascontiguousarray(dD_in)))
    rD = run_bass_kernel_spmd(ncD, inD, core_ids=cores).results

    ybatch = [np.empty((D, N), BF) for _ in range(2)]
    for kcore in cores:
        yk = np.asarray(rD[kcore]["y_out"])
        for j in range(4):
            b, blk = j // 2, j % 2
            rows = slice(kcore * D2 + blk * 128, kcore * D2 + (blk + 1) * 128)
            ybatch[b][rows, :] = yk[j * 128:(j + 1) * 128, :]
    wguT = np.ascontiguousarray(wgu.T)
    Wgup = np.ascontiguousarray(np.concatenate(
        [retile_cols(np.ascontiguousarray(wguT[:, :8192]), 128).reshape(64, 128, D),
         retile_cols(np.ascontiguousarray(wguT[:, 8192:]), 128).reshape(64, 128, D)],
        axis=2).reshape(64 * 128, 2 * D))
    wdnT = np.ascontiguousarray(wdn.T)
    Wdnp = np.ascontiguousarray(
        wdnT.reshape(64, 128, 4, 512).transpose(2, 0, 1, 3).reshape(-1, 512))
    ncE = _get("E", build_launch_E)
    inE = []
    for c in cores:
        b, s = spans[c]
        y_tm = np.ascontiguousarray(ybatch[b][:, s:s + SPAN].T)
        inE.append(dict(x2t=x2s[c], y_tm=y_tm, Wgup=Wgup,
                        Wdnp=Wdnp, ident_b=consts["ident_b"]))
    rE = run_bass_kernel_spmd(ncE, inE, core_ids=cores).results

    out = np.empty((B, N, D), np.float32)
    for c in cores:
        b, s = spans[c]
        out[b, s:s + SPAN] = np.asarray(rE[c]["out_o"])
    return out
